# version 07
# baseline (speedup 1.0000x reference)
import contextlib
import numpy as np
import ml_dtypes
import concourse.bass as bass
import concourse.mybir as mybir
from concourse.bass_utils import run_bass_kernel_spmd

F32 = mybir.dt.float32
BF16 = mybir.dt.bfloat16
U32 = mybir.dt.uint32
AF = mybir.ActivationFunctionType
ALU = mybir.AluOpType
TENSOR, VECTOR, SCALAR, SYNC, GPSIMD = "tensor", "vector", "scalar", "sync", "gpsimd"
ENGINES = [TENSOR, VECTOR, SCALAR, SYNC, GPSIMD]

B, T, D = 2, 8192, 1024
SPAN, HALO = 2048, 128
NT = SPAN + HALO
NTILE = NT // 128
CH = 512
DK, NH, DC, SC = 256, 4, 128, 1024
EPS = 1e-5

BUILD_PHASES = 99
DEV_CHUNK_LIMIT = None
DEV_B_CHUNKS = 16
DEBUG_TAPS = []


class Sem:
    __slots__ = ("name", "val", "handle")

    def __init__(self, name):
        self.name, self.val, self.handle = name, 0, None


class P:
    def __init__(self, nc):
        self.nc = nc
        self.ops = []
        self.sems = []
        self.bar = self.sem("BAR")

    def sem(self, name):
        s = Sem(f"{name}_{len(self.sems)}")
        self.sems.append(s)
        return s

    def op(self, engine, fn, wait=(), inc=()):
        self.ops.append((engine, fn, list(wait), list(inc)))

    def barrier(self):
        tgt = self.bar.val + len(ENGINES)
        self.bar.val = tgt
        for e in ENGINES:
            self.op(e, lambda eng: eng.nop(), inc=[(self.bar, 1)])
        for e in ENGINES:
            self.op(e, None, wait=[(self.bar, tgt)])

    def dma(self, engine, out, in_, sem, wait=()):
        v = sem.val + 16
        sem.val = v
        self.op(engine, lambda eng, out=out, in_=in_: eng.dma_start(out=out, in_=in_),
                wait=wait, inc=[(sem, 16)])
        return v

    def mm(self, out, lhsT, rhs, start, stop, wait=(), inc=()):
        self.op(TENSOR, lambda eng, out=out, l=lhsT, r=rhs, a=start, b=stop:
                eng.matmul(out, l, r, start=a, stop=b), wait=wait, inc=inc)

    def tr(self, out, in_, ident, wait=(), inc=()):
        self.op(TENSOR, lambda eng, out=out, i=in_, d=ident:
                eng.transpose(out, i, d), wait=wait, inc=inc)

    def ve(self, fn, wait=(), inc=()):
        self.op(VECTOR, fn, wait=wait, inc=inc)

    def sc(self, fn, wait=(), inc=()):
        self.op(SCALAR, fn, wait=wait, inc=inc)

    def gp(self, fn, wait=(), inc=()):
        self.op(GPSIMD, fn, wait=wait, inc=inc)

    def check(self):
        qs = {e: [o for o in self.ops if o[0] == e] for e in ENGINES}
        pos = {e: 0 for e in ENGINES}
        vals = {s.name: 0 for s in self.sems}
        prog = True
        while prog:
            prog = False
            for e in ENGINES:
                while pos[e] < len(qs[e]):
                    _, fn, wait, inc = qs[e][pos[e]]
                    if all(vals[s.name] >= v for s, v in wait):
                        for s, n in inc:
                            vals[s.name] += n
                        pos[e] += 1
                        prog = True
                    else:
                        break
        stuck = {e: pos[e] for e in ENGINES if pos[e] < len(qs[e])}
        if stuck:
            msgs = []
            for e, i in stuck.items():
                _, fn, wait, inc = qs[e][i]
                bad = [(s.name, v, vals[s.name]) for s, v in wait if vals[s.name] < v]
                msgs.append(f"{e} op#{i}/{len(qs[e])} waits {bad}")
            raise RuntimeError("DEADLOCK: " + "; ".join(msgs))

    def emit(self):
        self.check()
        nc = self.nc
        with contextlib.ExitStack() as est:
            for s in self.sems:
                s.handle = est.enter_context(nc.semaphore(s.name))
            with nc.Block() as block:
                for ename, meth in ((SYNC, block.sync), (TENSOR, block.tensor),
                                    (VECTOR, block.vector), (SCALAR, block.scalar),
                                    (GPSIMD, block.gpsimd)):
                    def body(eng, ename=ename):
                        waited = {}
                        for e, fn, wait, inc in self.ops:
                            if e != ename:
                                continue
                            for s, v in wait:
                                if ename == TENSOR and s.name.startswith("s_"):
                                    if v <= waited.get(s.name, -1):
                                        continue
                                    waited[s.name] = v
                                eng.wait_ge(s.handle, v)
                            inst = fn(eng) if fn is not None else None
                            if inst is None and inc:
                                inst = eng.nop()
                            for s, n in inc:
                                inst = inst.then_inc(s.handle, n)
                    meth(body)


def _conv_lhst(filt, ktaps):
    ntg = (ktaps + 3) // 4
    out = np.zeros((ntg, 2, 4, 128, 32), np.float32)
    cl = np.arange(32)
    for tg in range(ntg):
        for F in range(2):
            for gs in range(4):
                for sh_ in range(4):
                    tap = tg * 4 + sh_
                    if tap < ktaps:
                        out[tg, F, gs, sh_ * 32 + cl, cl] = \
                            filt[F * 128 + gs * 32 + cl, ktaps - 1 - tap]
    return np.ascontiguousarray(out.transpose(3, 0, 1, 2, 4).reshape(128, -1))


def _prep_inputs(I):
    sh = {}
    C = np.ascontiguousarray
    sh["aWq"] = C(I["L0_attn_a_Wq"])
    sh["aWdown"] = C(I["L0_attn_a_Wdown"])
    sh["aWup"] = C(I["L0_attn_a_Wup"])
    sh["aWout"] = C(I["L0_attn_a_Wout"])
    sh["a_bq_t"] = C(np.broadcast_to(I["L0_attn_a_bq"], (128, D)))
    sh["a_bup_t"] = C(np.broadcast_to(I["L0_attn_a_bup"], (128, 2 * D)))
    sh["a_bdown_c"] = C(I["L0_attn_a_bdown"].reshape(2, 128).T)
    sh["a_bout_c"] = C(I["L0_attn_a_bout"].reshape(8, 128).T)
    sh["fWr"] = C(I["L0_ffn_W_r"].T)
    sh["fWk"] = C(I["L0_ffn_W_k"].T)
    sh["fWv"] = C(I["L0_ffn_W_v"].T)
    sh["mu_r_c"] = C(I["L0_ffn_mu_r"].reshape(8, 128).T)
    sh["mu_k_c"] = C(I["L0_ffn_mu_k"].reshape(8, 128).T)
    sh["bWb"] = C(I["L0_attn_b_Wb"])
    sh["bWf1"] = C(I["L0_attn_b_Wf1"])
    sh["bWf2"] = C(I["L0_attn_b_Wf2"])
    sh["bWo"] = C(I["L0_attn_b_Wo"])
    sh["b_bf1_c"] = C(I["L0_attn_b_bf1"].reshape(16, 128).T)
    sh["b_bf2_c"] = C(I["L0_attn_b_bf2"].reshape(16, 1))
    sh["rms_w_c"] = C(I["L0_attn_b_rms_w"].reshape(2, 128).T)
    sh["eye_bf"] = np.eye(128, dtype=ml_dtypes.bfloat16)
    sh["eye_f32"] = np.eye(128, dtype=np.float32)
    ml_ = np.zeros((9, 128), np.float32)
    mr_ = np.zeros((9, 128), np.float32)
    ml_[0], mr_[0] = 16.0, -50.0
    for b_ in range(8):
        ml_[1 + b_, 16 * b_:16 * b_ + 16] = 16.0
        mr_[1 + b_, 16 * b_:16 * b_ + 16] = 50.0
    sh["maskL9"] = ml_.astype(ml_dtypes.bfloat16)
    sh["maskR9"] = mr_.astype(ml_dtypes.bfloat16)
    sh["onesrow_bf"] = np.ones((1, 128), ml_dtypes.bfloat16)
    sh["mask_ui"] = np.triu(np.ones((128, 128), np.float32))
    sh["mask_us"] = np.triu(np.ones((128, 128), np.float32), 1)
    sh["mask_ls"] = np.tril(np.ones((128, 128), np.float32), -1)
    hs = np.zeros((16, 4), np.float32)
    for hh in range(4):
        hs[4 * hh:4 * hh + 4, hh] = 1.0
    sh["hsel"] = hs
    sh["hexp"] = np.ascontiguousarray(hs.T)

    fir_s = I["L0_attn_b_fir_short_w"].reshape(D, 7)
    fir_l = I["L0_attn_b_fir_long_w"].reshape(D, 64)
    x = I["L0_attn_a_x"]
    per = []
    for c in range(8):
        g, s = c // 4, c % 4
        h = c % 4
        m = {}
        lo = s * SPAN - HALO
        if lo < 0:
            m["x_in"] = C(np.concatenate(
                [np.zeros((HALO, D), np.float32), x[g, 0:(s + 1) * SPAN]], axis=0))
        else:
            m["x_in"] = C(x[g, lo:(s + 1) * SPAN])
        m["halo_mask"] = np.full((128, 1), 0.0 if s == 0 else 1.0, np.float32)
        m["bWq_h"] = C(I["L0_attn_b_Wq"][:, h * DK:(h + 1) * DK])
        m["bWk_h"] = C(I["L0_attn_b_Wk"][:, h * DK:(h + 1) * DK])
        m["bWv_h"] = C(I["L0_attn_b_Wv"][:, h * DK:(h + 1) * DK])
        m["cw_q"] = _conv_lhst(I["L0_attn_b_conv_q_w"][h * DK:(h + 1) * DK], 4)
        m["cw_k"] = _conv_lhst(I["L0_attn_b_conv_k_w"][h * DK:(h + 1) * DK], 4)
        m["cw_v"] = _conv_lhst(I["L0_attn_b_conv_v_w"][h * DK:(h + 1) * DK], 4)
        m["cw_fs"] = _conv_lhst(fir_s[h * DK:(h + 1) * DK], 7)
        m["cw_fl"] = _conv_lhst(fir_l[h * DK:(h + 1) * DK], 64)
        selb = np.zeros((20, 1), np.float32)
        selb[16 + h, 0] = 1.0
        m["sel_beta"] = selb
        sf = np.zeros((20, 4), np.float32)
        for j in range(4):
            sf[4 * h + j, j] = 1.0
        m["sel_fw"] = sf
        m["wo_off"] = np.array([[s * SPAN]], dtype=np.uint32)
        per.append(m)
    return sh, per


IN_SPECS = [
    ("x_in", [NT, D], F32), ("halo_mask", [128, 1], F32),
    ("aWq", [D, D], F32), ("aWdown", [D, 256], F32), ("aWup", [256, 2 * D], F32),
    ("aWout", [D, D], F32), ("a_bq_t", [128, D], F32), ("a_bup_t", [128, 2 * D], F32),
    ("a_bdown_c", [128, 2], F32), ("a_bout_c", [128, 8], F32),
    ("fWr", [D, D], F32), ("fWk", [D, D], F32), ("fWv", [D, D], F32),
    ("mu_r_c", [128, 8], F32), ("mu_k_c", [128, 8], F32),
    ("bWb", [D, 4], F32), ("bWf1", [D, 2 * D], F32), ("bWf2", [2 * D, 16], F32),
    ("bWo", [D, D], F32), ("b_bf1_c", [128, 16], F32), ("b_bf2_c", [16, 1], F32),
    ("rms_w_c", [128, 2], F32),
    ("bWq_h", [D, DK], F32), ("bWk_h", [D, DK], F32), ("bWv_h", [D, DK], F32),
    ("cw_q", [128, 256], F32), ("cw_k", [128, 256], F32),
    ("cw_v", [128, 256], F32), ("cw_fs", [128, 512], F32),
    ("cw_fl", [128, 4096], F32),
    ("sel_beta", [20, 1], F32), ("sel_fw", [20, 4], F32),
    ("eye_f32", [128, 128], F32), ("mask_ui", [128, 128], F32),
    ("mask_us", [128, 128], F32), ("mask_ls", [128, 128], F32),
    ("hsel", [16, 4], F32), ("hexp", [4, 16], F32),
    ("eye_bf", [128, 128], BF16), ("maskL9", [9, 128], BF16),
    ("maskR9", [9, 128], BF16), ("onesrow_bf", [1, 128], BF16),
    ("ones_f", [128, 1], F32), ("onesrow_f", [1, 128], F32),
    ("cvals", [128, 8], F32),
    ("wo_off", [1, 1], U32),
]


def build_program():
    nc = bass.Bass("TRN2", target_bir_lowering=False, detect_race_conditions=False)
    io = {}
    for n, shp, dt in IN_SPECS:
        io[n] = nc.dram_tensor(n, list(shp), dt, kind="ExternalInput")
    io["y"] = nc.dram_tensor("y", [SPAN, D], F32, kind="ExternalOutput")

    dr = {}
    dr["qtm"] = nc.dram_tensor("dr_qtm", [CH, D], BF16)
    dr["ktm"] = nc.dram_tensor("dr_ktm", [CH, D], BF16)
    dr["vtm"] = nc.dram_tensor("dr_vtm", [CH, D], BF16)
    dr["atm"] = nc.dram_tensor("dr_atm", [CH, D], BF16)
    dr["ag_in"] = nc.dram_tensor("ag_in", [1044, SPAN], BF16)
    dr["ag_out"] = nc.dram_tensor("ag_out", [4 * 1044, SPAN], BF16)
    dr["ago_in"] = nc.dram_tensor("ago_in", [DK, HALO + T], BF16)
    dr["wo_win"] = nc.dram_tensor("wo_win", [4 * DK, NT], BF16)
    dr["ago_out"] = nc.dram_tensor("ago_out", [4 * DK, HALO + T], BF16)

    p = P(nc)
    with contextlib.ExitStack() as st:
        sb = {}

        def sbuf(name, shape, dtype=F32):
            sb[name] = st.enter_context(nc.sbuf_tensor("sb_" + name, list(shape), dtype))
            return sb[name]

        def psum(name, shape, dtype=F32):
            sb[name] = st.enter_context(nc.psum_tensor("ps_" + name, list(shape), dtype))
            return sb[name]

        _build(nc, p, io, dr, sb, sbuf, psum)
        if DEBUG_TAPS:
            p.barrier()
            dbg = p.sem("dbg")
            for n, sn, pdim, cols in DEBUG_TAPS:
                src = sb[sn][0:pdim, 0:cols]
                io[n] = nc.dram_tensor(n, [pdim, cols], src.dtype,
                                       kind="ExternalOutput")
                p.dma(SYNC, io[n].ap(), src, dbg)
            p.op(SYNC, None, wait=[(dbg, dbg.val)])
        p.emit()
    return nc


def _build(nc, p, io, dr, sb, sbuf, psum):
    x = sbuf("x", [128, 8 * NT])
    xn = sbuf("xn", [128, 8 * (CH + 1)], BF16)
    stg = sbuf("stg", [128, 4096], BF16)
    stat = sbuf("stat", [128, 16])
    onesrow_f = sbuf("onesrow_fs", [1, 128])
    wpool = sbuf("wpool", [128, 24576], BF16)
    wstage = sbuf("wstage", [128, 2 * 1024])
    cst = sbuf("cst", [128, 384], BF16)
    eye_f = sbuf("eye_f", [128, 128])
    maskL = sbuf("maskL", [9, 128], BF16)
    maskR = sbuf("maskR", [9, 128], BF16)
    mask_ui = sbuf("mask_ui", [128, 128])
    mask_us = sbuf("mask_us", [128, 128])
    mask_ls = sbuf("mask_ls", [128, 128])
    small = sbuf("small", [128, 80])
    tail = sbuf("tail", [128, 8], BF16)
    zrec = sbuf("zrec", [128, 64])
    cvals = sbuf("cvals", [128, 8])
    ones512 = sbuf("ones512", [1, 512])
    nc.const_aps.register(F32, EPS, cvals[:, 0:1])
    nc.const_aps.register(F32, 0.125, cvals[:, 1:2])
    nc.const_aps.register(F32, 1.0, cvals[:, 2:3])
    nc.const_aps.register(F32, 0.0, cvals[:, 3:4])
    nc.const_aps.register(F32, 1.0 / 256.0, cvals[:, 4:5])
    arena = sbuf("arena", [128, 30 * 1024], BF16)
    sb["srow"] = wstage[0:1, 0:1536]
    pb = [psum(f"pb{i}", [128, 512]) for i in range(6)]
    pt = [psum(f"pt{i}", [128, 1024], BF16) for i in range(2)]

    def bind(table):
        for nm, spec in table.items():
            off, cols = spec[0], spec[1]
            parts = spec[2] if len(spec) > 2 else 128
            dt_ = spec[3] if len(spec) > 3 else None
            v = arena[0:parts, off:off + cols]
            if dt_ is not None:
                v = v.bitcast(dt_)
            sb[nm] = v

    d_in = p.sem("d_in"); d_w = p.sem("d_w"); d_sc = p.sem("d_sc")
    s_pe = p.sem("s_pe"); s_ve = p.sem("s_ve"); s_se = p.sem("s_se")
    s_gp = p.sem("s_gp"); s_cc = p.sem("s_cc"); d_out = p.sem("d_out")
    d_ag = p.sem("d_ag")

    ctx = dict(nc=nc, p=p, io=io, dr=dr, sb=sb, pb=pb, pt=pt, bind=bind,
               d_in=d_in, d_w=d_w, d_sc=d_sc, s_pe=s_pe, s_ve=s_ve, s_se=s_se,
               s_gp=s_gp, s_cc=s_cc, d_out=d_out, d_ag=d_ag)

    phase_consts(ctx)
    if BUILD_PHASES < 1:
        return
    phase_xload(ctx)
    p.barrier()
    if BUILD_PHASES < 2:
        return
    phase_attn_a(ctx)
    p.barrier()
    if BUILD_PHASES < 3:
        return
    phase_ffn(ctx, 1)
    p.barrier()
    if BUILD_PHASES == 35:
        phase_out(ctx)
        return
    if BUILD_PHASES < 4:
        return
    phase_fw_ag(ctx)
    if BUILD_PHASES == 45:
        dbgs = p.sem("dbg_ag")
        p.op(GPSIMD, None, wait=[(ctx["s_cc"], ctx["s_cc"].val)])
        p.dma(GPSIMD, sb["arena"][0:128, 0:512],
              dr["ag_in"].ap()[0:128, 1536:2048], dbgs)
        p.dma(GPSIMD, sb["arena"][0:128, 512:1024],
              dr["ag_out"].ap()[3 * 1044:3 * 1044 + 128, 1536:2048], dbgs)
        p.dma(GPSIMD, sb["arena"][0:128, 1024:1536],
              dr["ag_out"].ap()[0 * 1044:0 * 1044 + 128, 0:512], dbgs)
        p.dma(GPSIMD, sb["arena"][0:128, 1536:2048],
              dr["ag_in"].ap()[0:128, 0:512], dbgs)
        p.op(GPSIMD, None, wait=[(dbgs, dbgs.val)])
        return
    if BUILD_PHASES < 5:
        return
    phase_attn_b(ctx)
    if BUILD_PHASES < 6:
        return
    phase_wo(ctx)
    p.barrier()
    if BUILD_PHASES < 7:
        return
    phase_ffn(ctx, 2)
    p.barrier()
    if BUILD_PHASES < 8:
        return
    phase_out(ctx)


CST = {"eye_bf": (0, 128), "onescol": (128, 1), "selb": (132, 1),
       "selfw": (136, 4), "onesrow": (144, 128)}
SMALL = {"a_bdown_c": (0, 2), "a_bout_c": (2, 8), "mu_r_c": (10, 8),
         "mu_k_c": (18, 8), "b_bf1_c": (26, 16), "rms_w_c": (42, 2),
         "halo_mask": (44, 1), "b_bf2_c": (45, 1)}


def phase_consts(ctx):
    p, io, sb = ctx["p"], ctx["io"], ctx["sb"]
    d_in, s_ve = ctx["d_in"], ctx["s_ve"]
    for nm, dst in [("eye_bf", sb["cst"][:, 0:128]),
                    ("onesrow_bf", sb["cst"][0:1, 144:272])]:
        p.dma(SYNC, dst, io[nm].ap(), d_in)
    p.dma(SYNC, sb["maskL"][:, :], io["maskL9"].ap(), d_in)
    p.dma(SYNC, sb["maskR"][:, :], io["maskR9"].ap(), d_in)
    p.dma(SYNC, sb["eye_f"][:, :], io["eye_f32"].ap(), d_in)
    p.dma(SYNC, sb["mask_ui"][:, :], io["mask_ui"].ap(), d_in)
    p.dma(SYNC, sb["mask_us"][:, :], io["mask_us"].ap(), d_in)
    p.dma(SYNC, sb["mask_ls"][:, :], io["mask_ls"].ap(), d_in)
    p.dma(SYNC, sb["small"][0:16, 46:50], io["hsel"].ap(), d_in)
    p.dma(SYNC, sb["small"][0:4, 50:66], io["hexp"].ap(), d_in)
    for nm, (c0, w) in SMALL.items():
        if nm == "b_bf2_c":
            p.dma(SYNC, sb["small"][0:16, c0:c0 + w], io[nm].ap(), d_in)
        else:
            p.dma(SYNC, sb["small"][:, c0:c0 + w], io[nm].ap(), d_in)
    p.dma(SYNC, sb["onesrow_fs"][0:1, :], io["onesrow_f"].ap(), d_in)
    p.dma(SYNC, sb["cvals"][:, :], io["cvals"].ap(), d_in)
    p.dma(SYNC, sb["wstage"][0:20, 0:1], io["sel_beta"].ap(), d_in)
    v = p.dma(SYNC, sb["wstage"][0:20, 4:8], io["sel_fw"].ap(), d_in)
    s_ve = ctx["s_ve"]
    p.ve(lambda eng: eng.memset(sb["cst"][:, 129:144], 0.0),
         inc=[(s_ve, _inc(s_ve))])
    p.ve(lambda eng: eng.memset(sb["xn"][:, :], 0.0), inc=[(s_ve, _inc(s_ve))])
    p.ve(lambda eng: eng.memset(sb["ones512"][0:1, :], 1.0), inc=[(s_ve, _inc(s_ve))])
    p.ve(lambda eng: eng.memset(sb["cst"][:, 128:129], 1.0),
         inc=[(s_ve, _inc(s_ve))])
    p.ve(lambda eng: eng.tensor_copy(sb["cst"][0:20, 132:133], sb["wstage"][0:20, 0:1]),
         wait=[(d_in, v)], inc=[(s_ve, _inc(s_ve))])
    p.ve(lambda eng: eng.tensor_copy(sb["cst"][0:20, 136:140], sb["wstage"][0:20, 4:8]),
         wait=[(d_in, v)], inc=[(s_ve, _inc(s_ve))])


def _inc(s, n=1):
    s.val += n
    return n


def _w(s):
    return (s, s.val)


def phase_xload(ctx):
    p, io, sb, pb = ctx["p"], ctx["io"], ctx["sb"], ctx["pb"]
    d_in, s_pe, s_ve = ctx["d_in"], ctx["s_pe"], ctx["s_ve"]
    eyef = sb["eye_f"][:, :]
    tdone = [0, 0]
    for t in range(NTILE):
        half = t % 2
        wst = sb["wstage"][:, half * 1024:(half + 1) * 1024]
        v = p.dma(SYNC, wst, io["x_in"].ap()[t * 128:(t + 1) * 128, :], d_in,
                  wait=[(s_pe, tdone[half])])
        for ct in range(8):
            bank = pb[ct % 2]
            p.tr(bank[:, 0:128].bitcast(F32), wst[:, ct * 128:(ct + 1) * 128], eyef,
                 wait=[(d_in, v), (s_ve, s_ve.val)], inc=[(s_pe, _inc(s_pe))])
            p.ve(lambda eng, ct=ct, t=t, bank=bank: eng.tensor_copy(
                sb["x"][:, ct * NT + t * 128: ct * NT + (t + 1) * 128], bank[:, 0:128]),
                wait=[(s_pe, s_pe.val)], inc=[(s_ve, _inc(s_ve))])
        tdone[half] = s_pe.val


class WPool:
    def __init__(self, ctx):
        self.ctx = ctx
        self.slots = {}
        self.col = 0

    def load(self, name, src, kt_n, cols):
        ctx, p, sb = self.ctx, self.ctx["p"], self.ctx["sb"]
        d_w, s_ve = ctx["d_w"], ctx["s_ve"]
        c0 = self.col
        self.slots[name] = (c0, cols)
        self.col += kt_n * cols
        assert self.col <= 24576, self.col
        for kt in range(kt_n):
            for j0 in range(0, cols, 1024):
                w = min(1024, cols - j0)
                half = (kt + j0 // 1024) % 2
                wst = sb["wstage"][:, half * 1024: half * 1024 + w]
                v = p.dma(SYNC, wst, src[kt * 128:(kt + 1) * 128, j0:j0 + w],
                          d_w, wait=[(s_ve, s_ve.val)])
                p.ve(lambda eng, c0=c0, kt=kt, j0=j0, w=w, wst=wst, cols=cols:
                     eng.tensor_copy(
                         sb["wpool"][:, c0 + kt * cols + j0: c0 + kt * cols + j0 + w],
                         wst), wait=[(d_w, v)], inc=[(s_ve, _inc(s_ve))])

    def t(self, name, kt, j0=0, w=None):
        c0, cols = self.slots[name]
        w = cols if w is None else w
        return self.ctx["sb"]["wpool"][:, c0 + kt * cols + j0: c0 + kt * cols + j0 + w]

    def reset(self):
        self.slots = {}
        self.col = 0



def _inc(s, n=1):
    s.val += n
    return n


class WPool:
    def __init__(self, ctx):
        self.ctx = ctx
        self.slots = {}
        self.col = 0

    def load(self, name, src, kt_n, cols):
        ctx, p, sb = self.ctx, self.ctx["p"], self.ctx["sb"]
        d_w, s_ve = ctx["d_w"], ctx["s_ve"]
        c0 = self.col
        self.slots[name] = (c0, cols)
        self.col += kt_n * cols
        assert self.col <= 24576, self.col
        for kt in range(kt_n):
            for j0 in range(0, cols, 1024):
                w = min(1024, cols - j0)
                half = ((kt * ((cols + 1023) // 1024)) + j0 // 1024) % 2
                wst = sb["wstage"][:, half * 1024: half * 1024 + w]
                v = p.dma(SYNC, wst, src[kt * 128:(kt + 1) * 128, j0:j0 + w],
                          d_w, wait=[(s_ve, s_ve.val)])
                p.ve(lambda eng, c0=c0, kt=kt, j0=j0, w=w, wst=wst, cols=cols:
                     eng.tensor_copy(
                         sb["wpool"][:, c0 + kt * cols + j0: c0 + kt * cols + j0 + w],
                         wst), wait=[(d_w, v)], inc=[(s_ve, _inc(s_ve))])

    def t(self, name, kt, j0=0, w=None):
        c0, cols = self.slots[name]
        w = cols - j0 if w is None else w
        return self.ctx["sb"]["wpool"][:, c0 + kt * cols + j0: c0 + kt * cols + j0 + w]

    def reset(self):
        self.slots = {}
        self.col = 0


def SMALLC(ctx, nm, i=0):
    c0, w = SMALL[nm]
    return ctx["sb"]["small"][:, c0 + i:c0 + i + 1]


def emit_ln(ctx, c0, W, apply_wait=()):
    p, sb, pb = ctx["p"], ctx["sb"], ctx["pb"]
    s_pe, s_ve, s_se = ctx["s_pe"], ctx["s_ve"], ctx["s_se"]
    onescol = sb["cst"][:, 128:129]
    pe_after = {}
    for ct in range(8):
        sl = (ct % 2) * CH
        xt = sb["x"][:, ct * NT + c0: ct * NT + c0 + W]
        wt = [(s_pe, pe_after[ct - 2])] if ct >= 2 else \
            [(s_pe, s_pe.val), (ctx["d_sc"], ctx["d_sc"].val),
             (ctx["d_ag"], ctx["d_ag"].val)]
        p.ve(lambda eng, xt=xt, sl=sl, W=W:
             eng.tensor_copy(sb["stg"][:, sl:sl + W], xt),
             wait=wt, inc=[(s_ve, _inc(s_ve))])
        p.sc(lambda eng, sl=sl, W=W: eng.activation(
            sb["stg"][:, 2048 + sl:2048 + sl + W], sb["stg"][:, sl:sl + W],
            AF.Square),
            wait=[(s_ve, s_ve.val)] + ([(s_pe, pe_after[ct - 2])] if ct >= 2 else []),
            inc=[(s_se, _inc(s_se))])
        p.mm(pb[4][0:1, 0:W], onescol, sb["stg"][:, sl:sl + W],
             start=(ct == 0), stop=(ct == 7),
             wait=[(s_ve, s_ve.val)],
             inc=[(s_pe, _inc(s_pe))])
        p.mm(pb[5][0:1, 0:W], onescol, sb["stg"][:, 2048 + sl:2048 + sl + W],
             start=(ct == 0), stop=(ct == 7), wait=[(s_se, s_se.val)],
             inc=[(s_pe, _inc(s_pe))])
        pe_after[ct] = s_pe.val
    vsum = s_pe.val
    sr = sb["srow"]
    p.ve(lambda eng, W=W: eng.tensor_copy(sr[0:1, 0:W], pb[4][0:1, 0:W]),
         wait=[(s_pe, vsum)], inc=[(s_ve, _inc(s_ve))])
    p.ve(lambda eng, W=W: eng.tensor_copy(sr[0:1, 512:512 + W], pb[5][0:1, 0:W]),
         wait=[(s_pe, vsum)], inc=[(s_ve, _inc(s_ve))])
    c = 1.0 / 1024.0
    p.ve(lambda eng, W=W: eng.tensor_scalar(sr[0:1, 0:W], sr[0:1, 0:W], c, None,
                                            ALU.mult), inc=[(s_ve, _inc(s_ve))])
    p.ve(lambda eng, W=W: eng.tensor_scalar(sr[0:1, 512:512 + W],
                                            sr[0:1, 512:512 + W], c, None,
                                            ALU.mult), inc=[(s_ve, _inc(s_ve))])
    p.ve(lambda eng, W=W: eng.tensor_tensor(sr[0:1, 1024:1024 + W], sr[0:1, 0:W],
                                            sr[0:1, 0:W], ALU.mult),
         inc=[(s_ve, _inc(s_ve))])
    p.ve(lambda eng, W=W: eng.tensor_tensor(sr[0:1, 512:512 + W],
                                            sr[0:1, 512:512 + W],
                                            sr[0:1, 1024:1024 + W], ALU.subtract),
         inc=[(s_ve, _inc(s_ve))])
    p.sc(lambda eng, W=W: eng.activation(sr[0:1, 1024:1024 + W],
                                         sr[0:1, 512:512 + W], AF.Ln, bias=EPS),
         wait=[(s_ve, s_ve.val)], inc=[(s_se, _inc(s_se))])
    p.sc(lambda eng, W=W: eng.activation(sr[0:1, 512:512 + W],
                                         sr[0:1, 1024:1024 + W], AF.Exp,
                                         scale=-0.5),
         inc=[(s_se, _inc(s_se))])
    p.ve(lambda eng, W=W: eng.tensor_tensor(sr[0:1, 0:W], sr[0:1, 0:W],
                                            sr[0:1, 512:512 + W], ALU.mult),
         wait=[(s_se, s_se.val)], inc=[(s_ve, _inc(s_ve))])
    vstat = s_ve.val
    orow = sb["onesrow_fs"]
    p.mm(pb[4][:, 0:W], orow[0:1, 0:128], sr[0:1, 512:512 + W], start=True, stop=True,
         wait=[(s_ve, vstat)], inc=[(s_pe, _inc(s_pe))])
    p.mm(pb[5][:, 0:W], orow[0:1, 0:128], sr[0:1, 0:W], start=True, stop=True,
         inc=[(s_pe, _inc(s_pe))])
    vb = s_pe.val
    for ct in range(8):
        sl = (ct % 2) * CH
        xt = sb["x"][:, ct * NT + c0: ct * NT + c0 + W]
        xnt = sb["xn"][:, ct * (CH + 1) + 1: ct * (CH + 1) + 1 + W]
        p.ve(lambda eng, xt=xt, sl=sl, W=W: eng.tensor_tensor(
            sb["stg"][:, sl:sl + W], xt, pb[4][:, 0:W], ALU.mult),
            wait=[(s_pe, vb)] + list(apply_wait), inc=[(s_ve, _inc(s_ve))])
        p.ve(lambda eng, xnt=xnt, sl=sl, W=W: eng.tensor_tensor(
            xnt, sb["stg"][:, sl:sl + W], pb[5][:, 0:W], ALU.subtract),
            inc=[(s_ve, _inc(s_ve))])


def XN(ctx, ct, t0, w):
    return ctx["sb"]["xn"][:, ct * (CH + 1) + 1 + t0: ct * (CH + 1) + 1 + t0 + w]


ATTN_A_TABLE = {
    "lat": (0, 1024), "qrow": (1024, 4096), "krow": (5120, 4096),
    "vaug": (9216, 4160), "qkd": (13376, 2048, 64), "ee": (15424, 1024),
    "asb": (16448, 4096), "atm": (20544, 2048), "afeat": (22592, 4096),
    "bias_bq": (26688, 1024), "bias_bup": (27712, 2048),
}


def phase_attn_a(ctx):
    p, sb, io = ctx["p"], ctx["sb"], ctx["io"]
    ctx["bind"](ATTN_A_TABLE)
    d_w, s_ve = ctx["d_w"], ctx["s_ve"]
    for nm, src, cols in [("bias_bq", "a_bq_t", 1024), ("bias_bup", "a_bup_t", 2048)]:
        for j0 in range(0, cols, 1024):
            half = (j0 // 1024) % 2
            wst = sb["wstage"][:, half * 1024: half * 1024 + 1024]
            v = p.dma(SYNC, wst, io[src].ap()[:, j0:j0 + 1024], d_w,
                      wait=[(s_ve, s_ve.val)])
            p.ve(lambda eng, nm=nm, j0=j0, wst=wst: eng.tensor_copy(
                sb[nm][:, j0:j0 + 1024], wst), wait=[(d_w, v)],
                inc=[(s_ve, _inc(s_ve))])
    wp = WPool(ctx)
    wp.load("aWq", io["aWq"].ap(), 8, D)
    wp.load("aWdown", io["aWdown"].ap(), 8, 256)
    wp.load("aWup", io["aWup"].ap(), 2, 2 * D)
    wp.load("aWout", io["aWout"].ap(), 8, D)
    p.barrier()
    chunks = [(i * CH, CH) for i in range(4)] + [(4 * CH, HALO)]
    if DEV_CHUNK_LIMIT is not None:
        chunks = chunks[:DEV_CHUNK_LIMIT]
    for (c0, W) in chunks:
        emit_ln(ctx, c0, W)
        emit_attn_a_chunk(ctx, wp, c0, W)
    wp.reset()


def emit_attn_a_chunk(ctx, wp, c0, W):
    p, sb, pb, pt, dr = ctx["p"], ctx["sb"], ctx["pb"], ctx["pt"], ctx["dr"]
    s_pe, s_ve, s_se = ctx["s_pe"], ctx["s_ve"], ctx["s_se"]
    d_sc, s_gp = ctx["d_sc"], ctx["s_gp"]
    ntt, ngrp = W // 128, W // 8
    eyebf = sb["cst"][:, 0:128]
    xn_ready = s_ve.val
    prev_rb = ctx.get("attn_a_prev_rb", [])

    for mt in range(2):
        bank = pb[2 + mt]
        for kt in range(8):
            p.mm(bank[:, 0:W], wp.t("aWdown", kt, mt * 128, 128),
                 XN(ctx, kt, 0, W), start=(kt == 0), stop=(kt == 7),
                 wait=[(s_ve, xn_ready)] if kt == 0 else (),
                 inc=[(s_pe, _inc(s_pe))] if kt == 7 else ())
        p.ve(lambda eng, mt=mt, bank=bank, W=W: eng.tensor_scalar(
            sb["lat"][:, mt * CH: mt * CH + W], bank[:, 0:W],
            SMALLC(ctx, "a_bdown_c", mt), None, ALU.add),
            wait=[(s_pe, s_pe.val)], inc=[(s_ve, _inc(s_ve))])
    lat_ready = s_ve.val

    dma_done = []
    slot_free = {0: (d_sc, d_sc.val), 1: (d_sc, d_sc.val)}
    for tt in range(ntt):
        sl = (tt % 2) * 2048
        for nj in range(2):
            bank = pb[nj]
            for kt in range(8):
                p.mm(bank[:, 0:512], XN(ctx, kt, tt * 128, 128),
                     wp.t("aWq", kt, nj * 512, 512),
                     start=(kt == 0), stop=(kt == 7),
                     wait=[(s_ve, s_ve.val), (s_gp, s_gp.val)] if kt == 0 else (),
                     inc=[(s_pe, _inc(s_pe))] if kt == 7 else ())
            p.ve(lambda eng, bank=bank, sl=sl, nj=nj: eng.tensor_tensor(
                sb["stg"][:, sl + nj * 512: sl + nj * 512 + 512], bank[:, 0:512],
                sb["bias_bq"][:, nj * 512:(nj + 1) * 512], ALU.add),
                wait=[(s_pe, s_pe.val), slot_free[tt % 2]],
                inc=[(s_ve, _inc(s_ve))])
        vq = p.dma(GPSIMD, dr["qtm"].ap()[tt * 128:(tt + 1) * 128, :],
                   sb["stg"][:, sl:sl + 1024], d_sc,
                   wait=[(s_ve, s_ve.val)] + (prev_rb if tt == 0 else []))
        for nj in range(4):
            bank = pb[nj]
            for kt in range(2):
                p.mm(bank[:, 0:512],
                     sb["lat"][:, kt * CH + tt * 128: kt * CH + tt * 128 + 128],
                     wp.t("aWup", kt, nj * 512, 512),
                     start=(kt == 0), stop=(kt == 1),
                     wait=[(s_ve, s_ve.val)] if kt == 0 else (),
                     inc=[(s_pe, _inc(s_pe))] if kt == 1 else ())
            p.ve(lambda eng, bank=bank, sl=sl, nj=nj: eng.tensor_tensor(
                sb["stg"][:, sl + nj * 512: sl + nj * 512 + 512],
                bank[:, 0:512], sb["bias_bup"][:, nj * 512:(nj + 1) * 512],
                ALU.add),
                wait=[(s_pe, s_pe.val)] + ([(d_sc, vq)] if nj < 2 else []),
                inc=[(s_ve, _inc(s_ve))])
        kv_view = sb["stg"][:, sl:sl + 2048].rearrange("p (h e) -> p h e", e=128)
        v = p.dma(GPSIMD, dr["ktm"].ap()[tt * 128:(tt + 1) * 128, :],
                  kv_view[:, :, 0:64], d_sc, wait=[(s_ve, s_ve.val)])
        v = p.dma(GPSIMD, dr["vtm"].ap()[tt * 128:(tt + 1) * 128, :],
                  kv_view[:, :, 64:128], d_sc, wait=[(s_ve, s_ve.val)])
        dma_done.append(v)
        slot_free[tt % 2] = (d_sc, v)

    p.ve(lambda eng, ngrp=ngrp: eng.memset(sb["vaug"][:, 0:ngrp * 65], 1.0),
         wait=[(s_pe, s_pe.val)], inc=[(s_ve, _inc(s_ve))])
    vaug_set = s_ve.val
    wv = [(d_sc, dma_done[-1])]
    src_q = dr["qtm"].ap()[0:W, :].rearrange("(g t) (h d) -> t h g d", t=8, h=16)
    src_k = dr["ktm"].ap()[0:W, :].rearrange("(g t) (h d) -> t h g d", t=8, h=16)
    src_v = dr["vtm"].ap()[0:W, :].rearrange("(g t) (h d) -> t h g d", t=8, h=16)
    v1 = p.dma(GPSIMD, sb["qrow"][:, 0:ngrp * 64], src_q, d_sc, wait=wv)
    v2 = p.dma(GPSIMD, sb["krow"][:, 0:ngrp * 64], src_k, d_sc, wait=wv)
    vaug_dst = sb["vaug"][:, 0:ngrp * 65].rearrange("p (g e) -> p g e", e=65)[:, :, 0:64]
    v3 = p.dma(GPSIMD, vaug_dst, src_v, d_sc, wait=wv + [(s_ve, vaug_set)])
    rows_ready = [(d_sc, v3)]

    for batch in range((ngrp + 7) // 8):
        gl = list(range(batch * 8, min(ngrp, batch * 8 + 8)))
        tw = rows_ready + [(s_ve, s_ve.val), (s_se, s_se.val)]
        for j, g in enumerate(gl):
            p.tr(pt[0][0:64, j * 128:(j + 1) * 128].bitcast(BF16),
                 sb["qrow"][:, g * 64:(g + 1) * 64], eyebf, wait=tw if j == 0 else (),
                 inc=[(s_pe, _inc(s_pe))])
            p.tr(pt[1][0:64, j * 128:(j + 1) * 128].bitcast(BF16),
                 sb["krow"][:, g * 64:(g + 1) * 64], eyebf,
                 inc=[(s_pe, _inc(s_pe))])
        vtr = s_pe.val
        p.ve(lambda eng: eng.tensor_copy(sb["qkd"][0:64, 0:1024], pt[0][0:64, :]),
             wait=[(s_pe, vtr)], inc=[(s_ve, _inc(s_ve))])
        p.ve(lambda eng: eng.tensor_copy(sb["qkd"][0:64, 1024:2048], pt[1][0:64, :]),
             wait=[(s_pe, vtr)], inc=[(s_ve, _inc(s_ve))])
        vqkd = s_ve.val
        for j, g in enumerate(gl):
            gbank = pb[(j // 4) % 2]
            p.mm(gbank[:, (j % 4) * 128:(j % 4 + 1) * 128],
                 sb["qkd"][0:64, 1024 + j * 128:1024 + (j + 1) * 128],
                 sb["qkd"][0:64, j * 128:(j + 1) * 128],
                 start=True, stop=False,
                 wait=[(s_ve, s_ve.val), (s_se, s_se.val)],
                 inc=())
            p.mm(gbank[:, (j % 4) * 128:(j % 4 + 1) * 128],
                 sb["maskL"][:, :], sb["maskR"][:, :], start=False, stop=True,
                 inc=[(s_pe, _inc(s_pe))] if j % 4 == 3 else ())
            if j % 4 == 3:
                gb = (j // 4) % 2
                p.sc(lambda eng, gb=gb, j=j: eng.activation(
                    sb["ee"][:, gb * 512:(gb + 1) * 512], pb[gb][:, 0:512],
                    AF.Exp, scale=0.125), wait=[(s_pe, s_pe.val), (s_ve, s_ve.val)],
                    inc=[(s_se, _inc(s_se))])
        for j, g in enumerate(gl):
            abank = pb[2 + (j // 4) % 2]
            p.mm(abank[:, (j % 4) * 128:(j % 4) * 128 + 65],
                 sb["ee"][:, ((j // 4) % 2) * 512 + (j % 4) * 128:
                          ((j // 4) % 2) * 512 + (j % 4 + 1) * 128],
                 sb["vaug"][:, g * 65:(g + 1) * 65],
                 start=True, stop=True,
                 wait=[(s_se, s_se.val), (s_ve, s_ve.val)] + rows_ready +
                 [(s_pe, s_pe.val)],
                 inc=[(s_pe, _inc(s_pe))])
            if j % 4 == 3:
                ab = 2 + (j // 4) % 2
                bi = batch * 8 + (j // 4) * 4
                asrc = pb[ab][:, 0:512].rearrange(
                    "p (g e) -> p g e", e=128)[:, :, 0:65]
                p.ve(lambda eng, asrc=asrc: eng.tensor_copy(
                    sb["wstage"][:, 0:260], asrc),
                    wait=[(s_pe, s_pe.val)], inc=[(s_ve, _inc(s_ve))])
                for jj in range(4):
                    p.ve(lambda eng, jj=jj: eng.tensor_copy(
                        sb["stat"][:, 8 + jj:9 + jj],
                        sb["wstage"][:, jj * 65 + 64:jj * 65 + 65]),
                        inc=[(s_ve, _inc(s_ve))])
                p.sc(lambda eng: eng.activation(
                    sb["stat"][:, 12:16], sb["stat"][:, 8:12], AF.Ln),
                    wait=[(s_ve, s_ve.val)], inc=[(s_se, _inc(s_se))])
                p.sc(lambda eng, bi=bi: eng.activation(
                    sb["zrec"][:, bi % 64:(bi % 64) + 4], sb["stat"][:, 12:16],
                    AF.Exp, scale=-1.0), inc=[(s_se, _inc(s_se))])
                for jj in range(4):
                    g2 = gl[(j // 4) * 4 + jj]
                    p.ve(lambda eng, jj=jj, g2=g2, bi=bi:
                         eng.tensor_scalar(
                             sb["asb"][:, g2 * 64:(g2 + 1) * 64],
                             sb["wstage"][:, jj * 65:jj * 65 + 64],
                             sb["zrec"][:, (bi % 64) + jj:(bi % 64) + jj + 1],
                             None, ALU.mult),
                         wait=[(s_se, s_se.val)] if jj == 0 else (),
                         inc=[(s_ve, _inc(s_ve))])

    dst_a = dr["atm"].ap()[0:W, :].rearrange("(g t) (h d) -> t h g d", t=8, h=16)
    va = p.dma(GPSIMD, dst_a, sb["asb"][:, 0:ngrp * 64], d_sc,
               wait=[(s_ve, s_ve.val)] + ctx.get("attn_a_prev_atm", []))
    for tt in range(ntt):
        sl = (tt % 2) * 1024
        va2 = p.dma(GPSIMD, sb["atm"][:, sl:sl + 1024],
                    dr["atm"].ap()[tt * 128:(tt + 1) * 128, :], d_sc,
                    wait=[(d_sc, va), (s_pe, s_pe.val)])
        for ct in range(8):
            p.tr(pt[1][:, ct * 128:(ct + 1) * 128].bitcast(BF16),
                 sb["atm"][:, sl + ct * 128: sl + (ct + 1) * 128], eyebf,
                 wait=[(d_sc, va2), (s_ve, s_ve.val)] if ct == 0 else (),
                 inc=[(s_pe, _inc(s_pe))])
        vtr = s_pe.val
        for ct in range(8):
            p.ve(lambda eng, ct=ct, tt=tt, W=W: eng.tensor_copy(
                sb["afeat"][:, ct * CH + tt * 128: ct * CH + (tt + 1) * 128],
                pt[1][:, ct * 128:(ct + 1) * 128]),
                wait=[(s_pe, vtr)] if ct == 0 else (), inc=[(s_ve, _inc(s_ve))])
    afeat_ready = s_ve.val
    ctx["attn_a_prev_rb"] = [(d_sc, v3)]
    ctx["attn_a_prev_atm"] = [(d_sc, d_sc.val)]

    for co in range(8):
        bank = pb[4 + co % 2]
        for kt in range(8):
            p.mm(bank[:, 0:W], wp.t("aWout", kt, co * 128, 128),
                 sb["afeat"][:, kt * CH: kt * CH + W],
                 start=(kt == 0), stop=(kt == 7),
                 wait=[(s_ve, afeat_ready)] if kt == 0 else (),
                 inc=[(s_pe, _inc(s_pe))] if kt == 7 else ())
        xt = sb["x"][:, co * NT + c0: co * NT + c0 + W]
        p.ve(lambda eng, bank=bank, xt=xt, co=co, W=W: eng.scalar_tensor_tensor(
            xt, bank[:, 0:W], SMALLC(ctx, "a_bout_c", co), xt,
            ALU.add, ALU.add), wait=[(s_pe, s_pe.val)], inc=[(s_ve, _inc(s_ve))])


FFN_TABLE = {"xr": (0, 4096), "xk": (4096, 4096), "rr": (8192, 4096),
             "kk": (12288, 4096)}


def phase_ffn(ctx, which):
    p, sb, io = ctx["p"], ctx["sb"], ctx["io"]
    pb = ctx["pb"]
    s_pe, s_ve, s_se = ctx["s_pe"], ctx["s_ve"], ctx["s_se"]
    ctx["bind"](FFN_TABLE)
    wp = WPool(ctx)
    wp.load("fWr", io["fWr"].ap(), 8, D)
    wp.load("fWk", io["fWk"].ap(), 8, D)
    wp.load("fWv", io["fWv"].ap(), 8, D)
    p.barrier()
    xnv = sb["xn"][:, :].rearrange("p (c w) -> p c w", w=CH + 1)
    chunks = [(i * CH, CH) for i in range(4)] + [(4 * CH, HALO)]
    for ci, (c0, W) in enumerate(chunks):
        if ci > 0:
            wp_ = chunks[ci - 1][1]
            for ct in range(8):
                p.ve(lambda eng, ct=ct, wp_=wp_: eng.tensor_copy(
                    sb["tail"][:, ct:ct + 1],
                    sb["xn"][:, ct * (CH + 1) + wp_:ct * (CH + 1) + wp_ + 1]),
                    inc=[(s_ve, _inc(s_ve))])
            for ct in range(8):
                p.ve(lambda eng, ct=ct: eng.tensor_copy(
                    sb["xn"][:, ct * (CH + 1):ct * (CH + 1) + 1],
                    sb["tail"][:, ct:ct + 1]),
                    inc=[(s_ve, _inc(s_ve))])
        emit_ln(ctx, c0, W)
        if ci == 0:
            for ct in range(8):
                p.ve(lambda eng, ct=ct: eng.tensor_scalar(
                    sb["xn"][:, ct * (CH + 1) + 128:ct * (CH + 1) + 129],
                    sb["xn"][:, ct * (CH + 1) + 128:ct * (CH + 1) + 129],
                    SMALLC(ctx, "halo_mask"), None, ALU.mult),
                    inc=[(s_ve, _inc(s_ve))])
        emit_ffn_chunk(ctx, wp, c0, W)
    wp.reset()


def emit_ffn_chunk(ctx, wp, c0, W):
    p, sb, pb = ctx["p"], ctx["sb"], ctx["pb"]
    s_pe, s_ve, s_se = ctx["s_pe"], ctx["s_ve"], ctx["s_se"]
    mu_r = lambda ct: SMALLC(ctx, "mu_r_c", ct)
    mu_k = lambda ct: SMALLC(ctx, "mu_k_c", ct)
    for ct in range(8):
        cur = XN(ctx, ct, 0, W)
        prev = sb["xn"][:, ct * (CH + 1) + 0: ct * (CH + 1) + W]
        dd = sb["stg"][:, (ct % 2) * CH:(ct % 2) * CH + W]
        p.ve(lambda eng, cur=cur, prev=prev, dd=dd: eng.tensor_tensor(
            dd, cur, prev, ALU.subtract),
            wait=[(s_pe, s_pe.val)] if ct < 2 else (),
            inc=[(s_ve, _inc(s_ve))])
        p.ve(lambda eng, ct=ct, dd=dd, prev=prev, W=W: eng.scalar_tensor_tensor(
            sb["xr"][:, ct * CH: ct * CH + W], dd, mu_r(ct), prev,
            ALU.mult, ALU.add), inc=[(s_ve, _inc(s_ve))])
        p.ve(lambda eng, ct=ct, dd=dd, prev=prev, W=W: eng.scalar_tensor_tensor(
            sb["xk"][:, ct * CH: ct * CH + W], dd, mu_k(ct), prev,
            ALU.mult, ALU.add), inc=[(s_ve, _inc(s_ve))])
    mix_ready = s_ve.val
    for co in range(8):
        bank = pb[co % 2]
        for kt in range(8):
            p.mm(bank[:, 0:W], wp.t("fWr", kt, co * 128, 128),
                 sb["xr"][:, kt * CH: kt * CH + W],
                 start=(kt == 0), stop=(kt == 7),
                 wait=[(s_ve, mix_ready), (s_se, s_se.val)] if kt == 0 else (),
                 inc=[(s_pe, _inc(s_pe))] if kt == 7 else ())
        p.sc(lambda eng, bank=bank, co=co, W=W: eng.activation(
            sb["rr"][:, co * CH: co * CH + W], bank[:, 0:W], AF.Sigmoid),
            wait=[(s_pe, s_pe.val)], inc=[(s_se, _inc(s_se))])
    for co in range(8):
        bank = pb[2 + co % 2]
        for kt in range(8):
            p.mm(bank[:, 0:W], wp.t("fWk", kt, co * 128, 128),
                 sb["xk"][:, kt * CH: kt * CH + W],
                 start=(kt == 0), stop=(kt == 7),
                 wait=[(s_ve, s_ve.val), (s_se, s_se.val)] if kt == 0 else (),
                 inc=[(s_pe, _inc(s_pe))] if kt == 7 else ())
        dd = sb["stg"][:, 2048 + (co % 2) * CH: 2048 + (co % 2) * CH + W]
        p.ve(lambda eng, bank=bank, dd=dd, W=W: eng.tensor_scalar(
            dd, bank[:, 0:W], 0.0, None, ALU.max),
            wait=[(s_pe, s_pe.val)], inc=[(s_ve, _inc(s_ve))])
        p.sc(lambda eng, co=co, dd=dd, W=W: eng.activation(
            sb["kk"][:, co * CH: co * CH + W], dd, AF.Square),
            wait=[(s_ve, s_ve.val)], inc=[(s_se, _inc(s_se))])
    kk_ready = s_se.val
    for co in range(8):
        bank = pb[co % 2]
        for kt in range(8):
            p.mm(bank[:, 0:W], wp.t("fWv", kt, co * 128, 128),
                 sb["kk"][:, kt * CH: kt * CH + W],
                 start=(kt == 0), stop=(kt == 7),
                 wait=[(s_se, kk_ready), (s_ve, s_ve.val)] if kt == 0 else (),
                 inc=[(s_pe, _inc(s_pe))] if kt == 7 else ())
        dd = sb["stg"][:, (co % 2) * CH:(co % 2) * CH + W]
        p.ve(lambda eng, bank=bank, dd=dd, co=co, W=W: eng.tensor_tensor(
            dd, bank[:, 0:W], sb["rr"][:, co * CH: co * CH + W], ALU.mult),
            wait=[(s_pe, s_pe.val)], inc=[(s_ve, _inc(s_ve))])
        xt = sb["x"][:, co * NT + c0: co * NT + c0 + W]
        p.ve(lambda eng, xt=xt, dd=dd: eng.tensor_tensor(xt, xt, dd, ALU.add),
             inc=[(s_ve, _inc(s_ve))])


FW_TABLE = {"h1": (0, 8192)}


def phase_fw_ag(ctx):
    p, sb, io, dr, pb = ctx["p"], ctx["sb"], ctx["io"], ctx["dr"], ctx["pb"]
    s_pe, s_ve, s_se, d_ag = ctx["s_pe"], ctx["s_ve"], ctx["s_se"], ctx["d_ag"]
    s_cc = ctx["s_cc"]
    ctx["bind"](FW_TABLE)
    wp = WPool(ctx)
    wp.load("bWf1", io["bWf1"].ap(), 8, 2 * D)
    wp.load("bWf2", io["bWf2"].ap(), 16, 16)
    wp.load("bWb", io["bWb"].ap(), 8, 4)
    p.barrier()
    E16 = sb["wstage"][0:16, 1536:2048]
    ZR = sb["wstage"][0:4, 1024:1536]
    for ci in range(4):
        c0, W = HALO + ci * CH, CH
        emit_ln(ctx, c0, W, apply_wait=[(d_ag, d_ag.val)])
        xn_ready = s_ve.val
        for ct in range(8):
            p.dma(SCALAR, dr["ag_in"].ap()[ct * 128:(ct + 1) * 128,
                                           c0 - HALO: c0 - HALO + W],
                  XN(ctx, ct, 0, W), d_ag, wait=[(s_ve, xn_ready)])
        for co in range(16):
            bank = pb[co % 4]
            for kt in range(8):
                p.mm(bank[:, 0:W], wp.t("bWf1", kt, co * 128, 128),
                     XN(ctx, kt, 0, W), start=(kt == 0), stop=(kt == 7),
                     wait=[(s_ve, xn_ready), (s_se, s_se.val)] if kt == 0 else (),
                     inc=[(s_pe, _inc(s_pe))] if kt == 7 else ())
            p.sc(lambda eng, bank=bank, co=co, W=W: eng.activation(
                sb["h1"][:, co * CH: co * CH + W], bank[:, 0:W], AF.Gelu,
                bias=SMALLC(ctx, "b_bf1_c", co)),
                wait=[(s_pe, s_pe.val)], inc=[(s_se, _inc(s_se))])
        h1_ready = s_se.val
        for kt in range(16):
            p.mm(pb[4][0:16, 0:W], wp.t("bWf2", kt, 0, 16),
                 sb["h1"][:, kt * CH: kt * CH + W],
                 start=(kt == 0), stop=(kt == 15),
                 wait=[(s_se, h1_ready), (s_ve, s_ve.val)] if kt == 0 else (),
                 inc=[(s_pe, _inc(s_pe))] if kt == 15 else ())
        p.sc(lambda eng, W=W: eng.activation(
            E16[:, 0:W], pb[4][0:16, 0:W], AF.Exp,
            bias=sb["small"][0:16, 45:46]),
            wait=[(s_pe, s_pe.val)], inc=[(s_se, _inc(s_se))])
        p.mm(pb[5][0:4, 0:W], sb["small"][0:16, 46:50], E16[:, 0:W],
             start=True, stop=True, wait=[(s_se, s_se.val), (s_ve, s_ve.val)],
             inc=[(s_pe, _inc(s_pe))])
        p.sc(lambda eng, W=W: eng.activation(
            sb["wstage"][32:36, 1536:1536 + W], pb[5][0:4, 0:W], AF.Ln),
            wait=[(s_pe, s_pe.val)], inc=[(s_se, _inc(s_se))])
        p.sc(lambda eng, W=W: eng.activation(
            ZR[:, 0:W], sb["wstage"][32:36, 1536:1536 + W], AF.Exp, scale=-1.0),
            inc=[(s_se, _inc(s_se))])
        p.mm(pb[4][0:16, 0:W], sb["small"][0:4, 50:66], ZR[:, 0:W],
             start=True, stop=True, wait=[(s_ve, s_ve.val), (s_se, s_se.val)],
             inc=[(s_pe, _inc(s_pe))])
        p.ve(lambda eng, W=W: eng.tensor_tensor(
            sb["stg"][0:16, 0:W], E16[:, 0:W], pb[4][0:16, 0:W], ALU.mult),
            wait=[(s_pe, s_pe.val)], inc=[(s_ve, _inc(s_ve))])
        vfw = p.dma(SCALAR, dr["ag_in"].ap()[1024:1040, c0 - HALO: c0 - HALO + W],
                    sb["stg"][0:16, 0:W], d_ag, wait=[(s_ve, s_ve.val)])
        for kt in range(8):
            p.mm(pb[5][0:4, 0:W], wp.t("bWb", kt, 0, 4), XN(ctx, kt, 0, W),
                 start=(kt == 0), stop=(kt == 7),
                 wait=[(s_ve, s_ve.val)] if kt == 0 else (),
                 inc=[(s_pe, _inc(s_pe))] if kt == 7 else ())
        p.sc(lambda eng, W=W: eng.activation(
            sb["stg"][32:36, 0:W], pb[5][0:4, 0:W], AF.Sigmoid),
            wait=[(s_pe, s_pe.val)], inc=[(s_se, _inc(s_se))])
        p.dma(SCALAR, dr["ag_in"].ap()[1040:1044, c0 - HALO: c0 - HALO + W],
              sb["stg"][32:36, 0:W], d_ag, wait=[(s_se, s_se.val)])
    wp.reset()
    p.barrier()
    p.gp(lambda eng: eng.collective_compute(
        "AllGather", mybir.AluOpType.bypass,
        replica_groups=[[0, 1, 2, 3], [4, 5, 6, 7]],
        ins=[ctx["dr"]["ag_in"].ap().opt()],
        outs=[ctx["dr"]["ag_out"].ap().opt()]),
        wait=[(d_ag, d_ag.val)], inc=[(s_cc, _inc(s_cc))])


ATTN_B_TABLE = {
    "agx": (0, 4096), "xc": (4096, 1032), "stck": (5128, 2320),
    "qf": (7448, 1024), "kf": (8472, 1024), "vf": (9496, 1152),
    "ktm": (10648, 512), "kbtm": (11160, 512), "vbtm": (11672, 512),
    "tch": (12184, 2560), "uu": (14744, 256), "wtf": (15000, 512),
    "utm": (15512, 256), "gt": (15768, 128),
    "dout": (15896, 1024), "shrt": (16920, 1024), "lng": (17944, 1024),
    "fwbt": (18968, 2048), "oh": (21016, 1024),
    "fwrows": (22040, 512, 20), "fwh": (22552, 512, 4),
    "beta_row": (23064, 2560, 1),
    "s_f32": (25624, 1024, 128, F32), "s_bf": (26648, 512),
    "cvt": (27160, 32),
}
TS = {n: i * 128 for i, n in enumerate(
    ["U", "L", "IU", "ILm", "U2", "L2", "U4", "L4", "U8", "L8", "U16", "L16",
     "U32", "L32", "U64", "L64", "P01", "P01T", "P23", "P23T"])}
TS["P45"] = TS["U2"]; TS["P45T"] = TS["L2"]
TS["P0123"] = TS["U4"]; TS["P0123T"] = TS["L4"]
TS["P456"] = TS["U8"]; TS["TmatT"] = TS["L8"]


def phase_attn_b(ctx):
    p, sb, io, dr, pb, pt = ctx["p"], ctx["sb"], ctx["io"], ctx["dr"], ctx["pb"], ctx["pt"]
    s_pe, s_ve, s_se, s_cc = ctx["s_pe"], ctx["s_ve"], ctx["s_se"], ctx["s_cc"]
    d_sc, d_ag = ctx["d_sc"], ctx["d_ag"]
    ctx["bind"](ATTN_B_TABLE)
    p.barrier()
    wp = WPool(ctx)
    wp.load("bWq_h", io["bWq_h"].ap(), 8, DK)
    wp.load("bWk_h", io["bWk_h"].ap(), 8, DK)
    wp.load("bWv_h", io["bWv_h"].ap(), 8, DK)
    for cw in ["cw_q", "cw_k", "cw_v", "cw_fs", "cw_fl"]:
        cols = io[cw].ap().shape[1]
        wp.load(cw, io[cw].ap(), 1, cols)
    p.barrier()

    def cwt(name, tg, F, gs):
        return wp.t(name, 0, ((tg * 2 + F) * 4 + gs) * 32, 32)

    p.ve(lambda eng: eng.memset(sb["cvt"][:, :], 0.0), inc=[(s_ve, _inc(s_ve))])
    p.ve(lambda eng: eng.memset(sb["stg"][:, 0:HALO], 0.0),
         inc=[(s_ve, _inc(s_ve))])
    for mt in range(2):
        p.dma(SCALAR, dr["ago_in"].ap()[mt * 128:(mt + 1) * 128, 0:HALO],
              sb["stg"][:, 0:HALO], ctx["d_ag"], wait=[(s_ve, s_ve.val)])
    p.ve(lambda eng: eng.memset(sb["vf"][:, :], 0.0), inc=[(s_ve, _inc(s_ve))])
    p.ve(lambda eng: eng.memset(sb["s_f32"][:, :], 0.0), inc=[(s_ve, _inc(s_ve))])
    p.ve(lambda eng: eng.memset(sb["s_bf"][:, :], 0.0), inc=[(s_ve, _inc(s_ve))])
    eyebf = sb["cst"][:, 0:128]
    onesrow = sb["cst"][0:1, 144:272]
    onescol = sb["cst"][:, 128:129]

    for sc in range(DEV_B_CHUNKS):
        r, rc = sc // 4, (sc % 4) * CH
        W = CH
        vagx = 0
        for kt in range(8):
            vagx = p.dma(SYNC, sb["agx"][:, kt * W:(kt + 1) * W],
                         dr["ag_out"].ap()[r * 1044 + kt * 128:
                                           r * 1044 + (kt + 1) * 128, rc:rc + W],
                         d_sc,
                         wait=([(s_cc, s_cc.val), (s_pe, s_pe.val),
                                (d_ag, d_ag.val)] if kt == 0 else ()))
        vrows = p.dma(SYNC, sb["fwrows"][:, 0:W],
                      dr["ag_out"].ap()[r * 1044 + 1024: r * 1044 + 1044, rc:rc + W],
                      d_sc, wait=[(s_pe, s_pe.val)])
        p.mm(pb[5][0:1, 0:W], sb["cst"][0:20, 132:133], sb["fwrows"][:, 0:W],
             start=True, stop=True,
             wait=[(d_sc, vrows), (s_ve, s_ve.val), (s_se, s_se.val)],
             inc=[(s_pe, _inc(s_pe))])
        p.ve(lambda eng, W=W: eng.tensor_copy(sb["wstage"][0:1, 0:W], pb[5][0:1, 0:W]),
             wait=[(s_pe, s_pe.val), (d_sc, d_sc.val)], inc=[(s_ve, _inc(s_ve))])
        p.ve(lambda eng, W=W: eng.tensor_copy(sb["beta_row"][0:1, 0:W],
                                              pb[5][0:1, 0:W]),
             inc=[(s_ve, _inc(s_ve))])
        for j in range(4):
            vbcol = p.dma(GPSIMD, sb["stat"][:, j:j + 1],
                          sb["wstage"][0:1, j * 128:(j + 1) * 128], d_sc,
                          wait=[(s_ve, s_ve.val)] if j == 0 else ())
        p.mm(pb[5][0:4, 0:W], sb["cst"][0:20, 136:140], sb["fwrows"][:, 0:W],
             start=True, stop=True, wait=[(s_ve, s_ve.val)],
             inc=[(s_pe, _inc(s_pe))])
        p.ve(lambda eng, W=W: eng.tensor_copy(sb["fwh"][:, 0:W], pb[5][0:4, 0:W]),
             wait=[(s_pe, s_pe.val), (d_sc, d_sc.val)], inc=[(s_ve, _inc(s_ve))])
        for j in range(4):
            p.dma(GPSIMD, sb["beta_row"][0:1, 512 + j * W: 512 + (j + 1) * W],
                  sb["fwh"][j:j + 1, 0:W], d_sc, wait=[(s_ve, s_ve.val)])

        for mt in range(2):
            p.ve(lambda eng, mt=mt: eng.tensor_copy(
                sb["vf"][:, mt * 576:mt * 576 + 64],
                sb["vf"][:, mt * 576 + 512:mt * 576 + 576]),
                wait=[(s_pe, s_pe.val)] if mt == 0 else (),
                inc=[(s_ve, _inc(s_ve))])

        for ti, (wn, cn, dstn) in enumerate([
                ("bWq_h", "cw_q", "qf"), ("bWk_h", "cw_k", "kf"),
                ("bWv_h", "cw_v", "vf")]):
            for mt in range(2):
                bank = pb[mt]
                for kt in range(8):
                    p.mm(bank[:, 0:W], wp.t(wn, kt, mt * 128, 128),
                         sb["agx"][:, kt * W:(kt + 1) * W],
                         start=(kt == 0), stop=(kt == 7),
                         wait=[(d_sc, vagx), (s_ve, s_ve.val)] if kt == 0 else (),
                         inc=[(s_pe, _inc(s_pe))] if kt == 7 else ())
                p.ve(lambda eng, bank=bank, mt=mt, W=W: eng.tensor_copy(
                    sb["xc"][:, mt * 516 + 4: mt * 516 + 4 + W], bank[:, 0:W]),
                    wait=[(s_pe, s_pe.val)], inc=[(s_ve, _inc(s_ve))])
                p.ve(lambda eng, ti=ti, mt=mt: eng.tensor_copy(
                    sb["xc"][:, mt * 516: mt * 516 + 4],
                    sb["cvt"][:, ti * 8 + mt * 4: ti * 8 + mt * 4 + 4]),
                    inc=[(s_ve, _inc(s_ve))])
                p.ve(lambda eng, ti=ti, mt=mt, W=W: eng.tensor_copy(
                    sb["cvt"][:, ti * 8 + mt * 4: ti * 8 + mt * 4 + 4],
                    sb["xc"][:, mt * 516 + W: mt * 516 + W + 4]),
                    inc=[(s_ve, _inc(s_ve))])
            xc_ready = s_ve.val
            for mt in range(2):
                for gs in range(4):
                    for sh in range(4):
                        src = sb["xc"][gs * 32:(gs + 1) * 32,
                                       mt * 516 + 4 - sh: mt * 516 + 4 - sh + W]
                        dstv = sb["stck"][sh * 32:(sh + 1) * 32,
                                          gs * 580: gs * 580 + W]
                        p.op(GPSIMD, lambda eng, src=src, dstv=dstv:
                             eng.dma_start(out=dstv, in_=src),
                             wait=[(s_ve, xc_ready), (s_pe, s_pe.val)]
                             if gs == 0 and sh == 0 else (),
                             inc=[(d_sc, 16)])
                        d_sc.val += 16
                vstk = d_sc.val
                bank = pb[2 + mt]
                for gs in range(4):
                    p.op(TENSOR, lambda eng, bank=bank, gs=gs, mt=mt, W=W,
                         lt=cwt(cn, 0, mt, gs):
                         eng.matmul(bank[gs * 32:(gs + 1) * 32, 0:W], lt,
                                    sb["stck"][:, gs * 580: gs * 580 + W],
                                    start=True, stop=True,
                                    tile_position=(0, gs * 32)),
                         wait=[(d_sc, vstk), (s_se, s_se.val)] if gs == 0 else (),
                         inc=[(s_pe, _inc(s_pe))] if gs == 3 else ())
                off = 64 if dstn == "vf" else 0
                wdt = 576 if dstn == "vf" else 512
                p.sc(lambda eng, bank=bank, dstn=dstn, mt=mt, off=off, wdt=wdt, W=W:
                     eng.activation(
                         sb[dstn][:, mt * wdt + off: mt * wdt + off + W],
                         bank[:, 0:W], AF.Silu),
                     wait=[(s_pe, s_pe.val)], inc=[(s_se, _inc(s_se))])
            if dstn != "vf":
                for mt in range(2):
                    p.sc(lambda eng, dstn=dstn, mt=mt, W=W: eng.activation(
                        sb["stg"][:, mt * CH: mt * CH + W],
                        sb[dstn][:, mt * 512: mt * 512 + W], AF.Square),
                        inc=[(s_se, _inc(s_se))])
                    p.mm(pb[4][0:1, 0:W], onescol, sb["stg"][:, mt * CH: mt * CH + W],
                         start=(mt == 0), stop=(mt == 1),
                         wait=[(s_se, s_se.val), (s_ve, s_ve.val)],
                         inc=[(s_pe, _inc(s_pe))] if mt == 1 else ())
                p.ve(lambda eng, W=W: eng.tensor_copy(sb["wstage"][0:1, 512:512 + W],
                                                      pb[4][0:1, 0:W]),
                     wait=[(s_pe, s_pe.val)], inc=[(s_ve, _inc(s_ve))])
                p.sc(lambda eng, W=W: eng.activation(
                    sb["wstage"][0:1, 1024:1024 + W], sb["wstage"][0:1, 512:512 + W],
                    AF.Ln), wait=[(s_ve, s_ve.val)], inc=[(s_se, _inc(s_se))])
                p.sc(lambda eng, W=W: eng.activation(
                    sb["wstage"][0:1, 512:512 + W], sb["wstage"][0:1, 1024:1024 + W],
                    AF.Exp, scale=-0.5), inc=[(s_se, _inc(s_se))])
                p.ve(lambda eng, W=W: eng.tensor_copy(
                    sb["stg"][0:1, 3072:3072 + W],
                    sb["wstage"][0:1, 512:512 + W]),
                    wait=[(s_se, s_se.val)], inc=[(s_ve, _inc(s_ve))])
                p.mm(pb[4][:, 0:W], onesrow, sb["stg"][0:1, 3072:3072 + W],
                     start=True, stop=True, wait=[(s_ve, s_ve.val)],
                     inc=[(s_pe, _inc(s_pe))])
                for mt in range(2):
                    p.ve(lambda eng, dstn=dstn, mt=mt, W=W: eng.tensor_tensor(
                        sb[dstn][:, mt * 512: mt * 512 + W],
                        sb[dstn][:, mt * 512: mt * 512 + W], pb[4][:, 0:W],
                        ALU.mult), wait=[(s_pe, s_pe.val)],
                        inc=[(s_ve, _inc(s_ve))])

        for mt in range(2):
            for gs in range(4):
                for sh in range(4):
                    src = sb["vf"][gs * 32:(gs + 1) * 32,
                                   mt * 576 + 4 - sh: mt * 576 + 4 - sh + 572]
                    dstv = sb["stck"][sh * 32:(sh + 1) * 32,
                                      gs * 580: gs * 580 + 572]
                    p.op(GPSIMD, lambda eng, src=src, dstv=dstv:
                         eng.dma_start(out=dstv, in_=src),
                         wait=[(s_se, s_se.val), (s_pe, s_pe.val)]
                         if gs == 0 and sh == 0 else (),
                         inc=[(d_sc, 16)])
                    d_sc.val += 16
            vstk = d_sc.val
            for nm, ntg, dstn in [("cw_fs", 2, "shrt"), ("cw_fl", 16, "lng")]:
                bank = pb[2 + mt]
                for gs in range(4):
                    for tg in range(ntg):
                        p.op(TENSOR,
                             lambda eng, bank=bank, gs=gs, tg=tg, mt=mt, W=W,
                             lt=cwt(nm, tg, mt, gs):
                             eng.matmul(bank[gs * 32:(gs + 1) * 32, 0:W],
                                        lt,
                                        sb["stck"][:, gs * 580 + 60 - 4 * tg:
                                                   gs * 580 + 60 - 4 * tg + W],
                                        start=(tg == 0), stop=(tg == ntg - 1),
                                        tile_position=(0, gs * 32)),
                             wait=[(d_sc, vstk), (s_ve, s_ve.val)]
                             if gs == 0 and tg == 0 else (),
                             inc=[(s_pe, _inc(s_pe))]
                             if (gs == 3 and tg == ntg - 1) else ())
                p.ve(lambda eng, bank=bank, dstn=dstn, mt=mt, W=W: eng.tensor_copy(
                    sb[dstn][:, mt * 512: mt * 512 + W], bank[:, 0:W]),
                    wait=[(s_pe, s_pe.val)], inc=[(s_ve, _inc(s_ve))])

        for dc in range(4):
            emit_delta_chunk(ctx, wp, sc, dc, vbcol)

        emit_mix_chunk(ctx, sc)

    p.barrier()
    p.gp(lambda eng: eng.collective_compute(
        "AllGather", mybir.AluOpType.bypass,
        replica_groups=[[0, 1, 2, 3], [4, 5, 6, 7]],
        ins=[dr["ago_in"].ap().opt()],
        outs=[dr["ago_out"].ap().opt()]),
        wait=[(d_ag, d_ag.val)], inc=[(s_cc, _inc(s_cc))])
    wp.reset()


def emit_delta_chunk(ctx, wp, sc, dc, vbcol):
    p, sb, pb, pt = ctx["p"], ctx["sb"], ctx["pb"], ctx["pt"]
    s_pe, s_ve, s_se, d_sc = ctx["s_pe"], ctx["s_ve"], ctx["s_se"], ctx["d_sc"]
    eyebf = sb["cst"][:, 0:128]
    tc = lambda n: sb["tch"][:, TS[n]: TS[n] + 128]
    bcol = sb["stat"][:, dc:dc + 1]
    qs = lambda mt: sb["qf"][:, mt * 512 + dc * 128: mt * 512 + (dc + 1) * 128]
    ks = lambda mt: sb["kf"][:, mt * 512 + dc * 128: mt * 512 + (dc + 1) * 128]
    vs = lambda mt: sb["vf"][:, mt * 576 + 64 + dc * 128:
                             mt * 576 + 64 + (dc + 1) * 128]

    for mt in range(2):
        p.tr(pt[0][:, mt * 128:(mt + 1) * 128].bitcast(BF16), ks(mt), eyebf,
             wait=[(s_ve, s_ve.val), (s_se, s_se.val)] if mt == 0 else (),
             inc=[(s_pe, _inc(s_pe))])
        p.tr(pt[0][:, 256 + mt * 128: 256 + (mt + 1) * 128].bitcast(BF16),
             vs(mt), eyebf, inc=[(s_pe, _inc(s_pe))])
    vtr = s_pe.val
    for mt in range(2):
        p.ve(lambda eng, mt=mt: eng.tensor_copy(
            sb["ktm"][:, mt * 128:(mt + 1) * 128],
            pt[0][:, mt * 128:(mt + 1) * 128]),
            wait=[(s_pe, vtr), (d_sc, vbcol)] if mt == 0 else (),
            inc=[(s_ve, _inc(s_ve))])
        p.ve(lambda eng, mt=mt: eng.tensor_scalar(
            sb["kbtm"][:, mt * 128:(mt + 1) * 128],
            pt[0][:, mt * 128:(mt + 1) * 128], bcol, None, ALU.mult),
            inc=[(s_ve, _inc(s_ve))])
        p.ve(lambda eng, mt=mt: eng.tensor_scalar(
            sb["vbtm"][:, mt * 128:(mt + 1) * 128],
            pt[0][:, 256 + mt * 128: 256 + (mt + 1) * 128], bcol, None,
            ALU.mult), inc=[(s_ve, _inc(s_ve))])

    for mt in range(2):
        p.mm(pb[0][:, 0:128], ks(mt), ks(mt), start=(mt == 0), stop=(mt == 1),
             wait=[(s_ve, s_ve.val)] if mt == 0 else (),
             inc=[(s_pe, _inc(s_pe))] if mt == 1 else ())
    for mt in range(2):
        p.mm(pb[1][:, 0:128], ks(mt), qs(mt), start=(mt == 0), stop=(mt == 1),
             wait=[(s_ve, s_ve.val)] if mt == 0 else (),
             inc=[(s_pe, _inc(s_pe))] if mt == 1 else ())
    vg = s_pe.val
    p.ve(lambda eng: eng.scalar_tensor_tensor(
        tc("L"), pb[0][:, 0:128], bcol, sb["mask_ls"][:, :], ALU.mult, ALU.mult),
        wait=[(s_pe, vg)], inc=[(s_ve, _inc(s_ve))])
    p.ve(lambda eng: eng.tensor_tensor(sb["gt"][:, 0:128], pb[1][:, 0:128],
                                       sb["mask_ui"][:, :], ALU.mult),
         wait=[(s_pe, vg)], inc=[(s_ve, _inc(s_ve))])
    p.tr(pt[0][:, 0:128].bitcast(BF16), tc("L"), eyebf,
         wait=[(s_ve, s_ve.val)], inc=[(s_pe, _inc(s_pe))])
    p.ve(lambda eng: eng.tensor_copy(tc("U"), pt[0][:, 0:128]),
         wait=[(s_pe, s_pe.val)], inc=[(s_ve, _inc(s_ve))])
    p.ve(lambda eng: eng.tensor_tensor(tc("IU"), eyebf, tc("U"), ALU.subtract),
         inc=[(s_ve, _inc(s_ve))])
    p.ve(lambda eng: eng.tensor_tensor(tc("ILm"), eyebf, tc("L"), ALU.subtract),
         inc=[(s_ve, _inc(s_ve))])

    def mmev(bank, mms, dst, add=None):
        n = len(mms)
        for i_, (l_, r_) in enumerate(mms):
            p.mm(bank[:, 0:128], l_, r_, start=(i_ == 0), stop=(i_ == n - 1),
                 wait=[(s_ve, s_ve.val), (s_se, s_se.val)] if i_ == 0 else (),
                 inc=[(s_pe, _inc(s_pe))] if i_ == n - 1 else ())
        if add is None:
            p.sc(lambda eng, bank=bank, dst=dst: eng.activation(
                dst, bank[:, 0:128], AF.Copy),
                wait=[(s_pe, s_pe.val)], inc=[(s_se, _inc(s_se))])
        else:
            p.ve(lambda eng, bank=bank, dst=dst, add=add: eng.tensor_tensor(
                dst, bank[:, 0:128], add, ALU.add),
                wait=[(s_pe, s_pe.val)], inc=[(s_ve, _inc(s_ve))])

    for (nu, nl, pu, pl) in [("U2", "L2", "U", "L"), ("U4", "L4", "U2", "L2"),
                             ("U8", "L8", "U4", "L4"),
                             ("U16", "L16", "U8", "L8"),
                             ("U32", "L32", "U16", "L16"),
                             ("U64", "L64", "U32", "L32")]:
        mmev(pb[0], [(tc(pl), tc(pu))], tc(nu))
        mmev(pb[1], [(tc(pu), tc(pl))], tc(nl))
    mmev(pb[0], [(tc("ILm"), tc("U2"))], tc("P01"), add=tc("IU"))
    mmev(pb[1], [(tc("U2"), tc("ILm"))], tc("P01T"), add=tc("ILm"))
    mmev(pb[0], [(tc("L4"), tc("U8")), (tc("L4"), eyebf), (eyebf, tc("U8"))],
         tc("P23"), add=eyebf)
    mmev(pb[1], [(tc("U8"), tc("L4")), (tc("U8"), eyebf), (eyebf, tc("L4"))],
         tc("P23T"), add=eyebf)
    mmev(pb[0], [(tc("L16"), tc("U32")), (tc("L16"), eyebf),
                 (eyebf, tc("U32"))], tc("P45"), add=eyebf)
    mmev(pb[1], [(tc("U32"), tc("L16")), (tc("U32"), eyebf),
                 (eyebf, tc("L16"))], tc("P45T"), add=eyebf)
    mmev(pb[0], [(tc("P01T"), tc("P23"))], tc("P0123"))
    mmev(pb[1], [(tc("P23"), tc("P01T"))], tc("P0123T"))
    mmev(pb[0], [(tc("P45T"), tc("U64"))], tc("P456"), add=tc("P45"))
    mmev(pb[1], [(tc("P0123T"), tc("P456"))], tc("TmatT"))

    p.mm(pb[2][:, 0:256], tc("TmatT"), sb["vbtm"][:, 0:256], start=True,
         stop=True, wait=[(s_ve, s_ve.val), (s_se, s_se.val)],
         inc=[(s_pe, _inc(s_pe))])
    p.ve(lambda eng: eng.tensor_copy(sb["uu"][:, 0:256], pb[2][:, 0:256]),
         wait=[(s_pe, s_pe.val)], inc=[(s_ve, _inc(s_ve))])
    for mt in range(2):
        p.mm(pb[3][:, mt * 128:(mt + 1) * 128],
             sb["kbtm"][:, mt * 128:(mt + 1) * 128], tc("TmatT"),
             start=True, stop=True,
             wait=[(s_ve, s_ve.val)] if mt == 0 else (),
             inc=[(s_pe, _inc(s_pe))] if mt == 1 else ())
    p.ve(lambda eng: eng.tensor_copy(sb["wtf"][:, 0:256], pb[3][:, 0:256]),
         wait=[(s_pe, s_pe.val)], inc=[(s_ve, _inc(s_ve))])

    for mt in range(2):
        p.mm(pb[2][:, 256:512], sb["wtf"][:, mt * 128:(mt + 1) * 128],
             sb["s_bf"][:, mt * 256:(mt + 1) * 256],
             start=(mt == 0), stop=(mt == 1),
             wait=[(s_ve, s_ve.val)] if mt == 0 else (),
             inc=[(s_pe, _inc(s_pe))] if mt == 1 else ())
    p.ve(lambda eng: eng.tensor_tensor(sb["utm"][:, 0:256], sb["uu"][:, 0:256],
                                       pb[2][:, 256:512], ALU.subtract),
         wait=[(s_pe, s_pe.val)], inc=[(s_ve, _inc(s_ve))])
    for mt in range(2):
        mms = [(sb["s_bf"][:, kt * 256 + mt * 128: kt * 256 + (mt + 1) * 128],
                qs(kt)) for kt in range(2)]
        mms.append((sb["utm"][:, mt * 128:(mt + 1) * 128], sb["gt"][:, 0:128]))
        for i_, (l_, r_) in enumerate(mms):
            p.mm(pb[3 + mt][:, 256:384], l_, r_, start=(i_ == 0),
                 stop=(i_ == 2),
                 wait=[(s_ve, s_ve.val)] if i_ == 0 else (),
                 inc=[(s_pe, _inc(s_pe))] if i_ == 2 else ())
    vo = s_pe.val
    for mt in range(2):
        p.ve(lambda eng, mt=mt: eng.tensor_copy(
            sb["dout"][:, mt * 512 + dc * 128: mt * 512 + (dc + 1) * 128],
            pb[3 + mt][:, 256:384]),
            wait=[(s_pe, vo)] if mt == 0 else (), inc=[(s_ve, _inc(s_ve))])
    for mt in range(2):
        p.mm(pb[5][:, mt * 256:(mt + 1) * 256],
             sb["ktm"][:, mt * 128:(mt + 1) * 128], sb["utm"][:, 0:256],
             start=True, stop=True,
             wait=[(s_ve, s_ve.val)] if mt == 0 else (),
             inc=[(s_pe, _inc(s_pe))] if mt == 1 else ())
    vs_ = s_pe.val
    for mt in range(2):
        p.ve(lambda eng, mt=mt: eng.tensor_tensor(
            sb["s_f32"][:, mt * 256:(mt + 1) * 256],
            sb["s_f32"][:, mt * 256:(mt + 1) * 256],
            pb[5][:, mt * 256:(mt + 1) * 256], ALU.add),
            wait=[(s_pe, vs_)] if mt == 0 else (), inc=[(s_ve, _inc(s_ve))])
    p.ve(lambda eng: eng.tensor_copy(sb["s_bf"][:, 0:512], sb["s_f32"][:, 0:512]),
         inc=[(s_ve, _inc(s_ve))])


def emit_mix_chunk(ctx, sc):
    p, sb, pb, dr = ctx["p"], ctx["sb"], ctx["pb"], ctx["dr"]
    s_pe, s_ve, s_se, d_ag = ctx["s_pe"], ctx["s_ve"], ctx["s_se"], ctx["d_ag"]
    onesrow = sb["cst"][0:1, 144:272]
    W = CH
    d_sc = ctx["d_sc"]
    for j in range(4):
        p.mm(pb[j][:, 0:W], onesrow, sb["beta_row"][0:1, 512 + j * W:
                                                    512 + (j + 1) * W],
             start=True, stop=True,
             wait=[(s_ve, s_ve.val), (s_se, s_se.val), (d_sc, d_sc.val)]
             if j == 0 else (),
             inc=[(s_pe, _inc(s_pe))] if j == 3 else ())
    vb = s_pe.val
    for j in range(4):
        p.ve(lambda eng, j=j, W=W: eng.tensor_copy(
            sb["fwbt"][:, j * W:(j + 1) * W], pb[j][:, 0:W]),
            wait=[(s_pe, vb)] if j == 0 else (), inc=[(s_ve, _inc(s_ve))])
    for mt in range(2):
        ohs = sb["oh"][:, mt * W:(mt + 1) * W]
        v4s = sb["vf"][:, mt * 576 + 64: mt * 576 + 64 + W]
        terms = [("shrt", 0), ("lng", 1), ("dout", 2)]
        p.ve(lambda eng, mt=mt, ohs=ohs, W=W: eng.tensor_tensor(
            ohs, sb["shrt"][:, mt * W:(mt + 1) * W], sb["fwbt"][:, 0:W],
            ALU.mult), wait=[(d_ag, d_ag.val)], inc=[(s_ve, _inc(s_ve))])
        for nm, j in [("lng", 1), ("dout", 2)]:
            tmp = sb["stg"][:, 0:W]
            p.ve(lambda eng, nm=nm, j=j, mt=mt, tmp=tmp, W=W: eng.tensor_tensor(
                tmp, sb[nm][:, mt * W:(mt + 1) * W],
                sb["fwbt"][:, j * W:(j + 1) * W], ALU.mult),
                inc=[(s_ve, _inc(s_ve))])
            p.ve(lambda eng, ohs=ohs, tmp=tmp: eng.tensor_tensor(
                ohs, ohs, tmp, ALU.add), inc=[(s_ve, _inc(s_ve))])
        tmp = sb["stg"][:, 0:W]
        p.ve(lambda eng, v4s=v4s, tmp=tmp, W=W: eng.tensor_tensor(
            tmp, v4s, sb["fwbt"][:, 3 * W:4 * W], ALU.mult),
            inc=[(s_ve, _inc(s_ve))])
        p.ve(lambda eng, ohs=ohs, tmp=tmp: eng.tensor_tensor(
            ohs, ohs, tmp, ALU.add), inc=[(s_ve, _inc(s_ve))])
    mix_done = s_ve.val
    for mt in range(2):
        p.sc(lambda eng, mt=mt, W=W: eng.activation(
            sb["stg"][:, 2048 + mt * CH:2048 + mt * CH + W],
            sb["oh"][:, mt * W:(mt + 1) * W],
            AF.Square), wait=[(s_ve, mix_done)], inc=[(s_se, _inc(s_se))])
        p.mm(pb[4][0:1, 0:W], sb["cst"][:, 128:129],
             sb["stg"][:, 2048 + mt * CH:2048 + mt * CH + W],
             start=(mt == 0), stop=(mt == 1),
             wait=[(s_se, s_se.val), (s_ve, s_ve.val)],
             inc=[(s_pe, _inc(s_pe))] if mt == 1 else ())
    p.ve(lambda eng, W=W: eng.tensor_copy(sb["wstage"][0:1, 1024:1024 + W],
                                          pb[4][0:1, 0:W]),
         wait=[(s_pe, s_pe.val)], inc=[(s_ve, _inc(s_ve))])
    p.sc(lambda eng, W=W: eng.activation(
        sb["wstage"][0:1, 1536:1536 + W], sb["wstage"][0:1, 1024:1024 + W],
        AF.Ln, bias=EPS, scale=1.0 / 256.0),
        wait=[(s_ve, s_ve.val)], inc=[(s_se, _inc(s_se))])
    p.sc(lambda eng, W=W: eng.activation(
        sb["wstage"][0:1, 1024:1024 + W], sb["wstage"][0:1, 1536:1536 + W],
        AF.Exp, scale=-0.5), inc=[(s_se, _inc(s_se))])
    p.ve(lambda eng, W=W: eng.tensor_copy(sb["stg"][0:1, 3584:3584 + W],
                                          sb["wstage"][0:1, 1024:1024 + W]),
         wait=[(s_se, s_se.val)], inc=[(s_ve, _inc(s_ve))])
    p.mm(pb[4][:, 0:W], onesrow, sb["stg"][0:1, 3584:3584 + W], start=True,
         stop=True, wait=[(s_ve, s_ve.val)], inc=[(s_pe, _inc(s_pe))])
    vbn = s_pe.val
    for mt in range(2):
        p.ve(lambda eng, mt=mt, W=W: eng.scalar_tensor_tensor(
            sb["oh"][:, mt * W:(mt + 1) * W], sb["oh"][:, mt * W:(mt + 1) * W],
            SMALLC(ctx, "rms_w_c", mt), pb[4][:, 0:W], ALU.mult, ALU.mult),
            wait=[(s_pe, vbn)] if mt == 0 else (), inc=[(s_ve, _inc(s_ve))])
    oh_done = s_ve.val
    for mt in range(2):
        p.dma(SCALAR, dr["ago_in"].ap()[mt * 128:(mt + 1) * 128,
                                        HALO + sc * CH: HALO + (sc + 1) * CH],
              sb["oh"][:, mt * W:(mt + 1) * W], d_ag, wait=[(s_ve, oh_done)])


WO_TABLE = {"obuf": (0, 4096)}


def phase_wo(ctx):
    p, sb, io, dr, pb = ctx["p"], ctx["sb"], ctx["io"], ctx["dr"], ctx["pb"]
    s_pe, s_ve, s_se, s_cc, d_sc = (ctx["s_pe"], ctx["s_ve"], ctx["s_se"],
                                    ctx["s_cc"], ctx["d_sc"])
    nc = ctx["nc"]
    ctx["bind"](WO_TABLE)
    p.barrier()
    wp = WPool(ctx)
    wp.load("bWo", io["bWo"].ap(), 8, D)
    p.barrier()
    cell = [None]

    def loadoff(eng):
        cell[0] = nc.values_load(io["wo_off"].ap()[0:1, 0:1],
                                 engines=[mybir.EngineType.Pool],
                                 min_val=0, max_val=HALO + T - NT,
                                 skip_runtime_bounds_check=True)
        return None
    p.op(GPSIMD, loadoff)

    def dmawin(eng):
        return eng.dma_start(out=dr["wo_win"].ap(),
                             in_=dr["ago_out"].ap()[:, bass.ds(cell[0], NT)])
    d_sc.val += 16
    vwin = d_sc.val
    p.op(GPSIMD, dmawin, wait=[(s_cc, s_cc.val)], inc=[(d_sc, 16)])
    chunks = [(i * CH, CH) for i in range(4)] + [(4 * CH, HALO)]
    for ci, (c0, W) in enumerate(chunks):
        vld = 0
        for kt in range(8):
            d_sc.val += 16
            vld = d_sc.val
            p.op(SYNC, lambda eng, kt=kt, c0=c0, W=W: eng.dma_start(
                out=sb["obuf"][:, kt * W:(kt + 1) * W],
                in_=dr["wo_win"].ap()[kt * 128:(kt + 1) * 128, c0:c0 + W]),
                wait=[(d_sc, vwin), (s_pe, s_pe.val)] if kt == 0 else (),
                inc=[(d_sc, 16)])
        for co in range(8):
            bank = pb[4 + co % 2]
            for kt in range(8):
                p.mm(bank[:, 0:W], wp.t("bWo", kt, co * 128, 128),
                     sb["obuf"][:, kt * W:(kt + 1) * W],
                     start=(kt == 0), stop=(kt == 7),
                     wait=[(d_sc, vld), (s_ve, s_ve.val)] if kt == 0 else (),
                     inc=[(s_pe, _inc(s_pe))] if kt == 7 else ())
            xt = sb["x"][:, co * NT + c0: co * NT + c0 + W]
            p.ve(lambda eng, bank=bank, xt=xt, W=W: eng.tensor_tensor(
                xt, xt, bank[:, 0:W], ALU.add),
                wait=[(s_pe, s_pe.val)], inc=[(s_ve, _inc(s_ve))])
    wp.reset()



def phase_out(ctx):
    p, sb, pb, pt, io = ctx["p"], ctx["sb"], ctx["pb"], ctx["pt"], ctx["io"]
    s_pe, s_ve, d_out = ctx["s_pe"], ctx["s_ve"], ctx["d_out"]
    eyef = sb["eye_f"][:, :]
    for t in range(HALO // 128, NTILE):
        sl = (t % 2) * 1024
        for ct in range(8):
            bank = pb[ct % 4]
            p.tr(bank[:, 0:128], sb["x"][:, ct * NT + t * 128: ct * NT + (t + 1) * 128],
                 eyef, wait=[(s_ve, s_ve.val), (ctx["d_out"], ctx["d_out"].val)],
                 inc=[(s_pe, _inc(s_pe))])
            p.ve(lambda eng, ct=ct, sl=sl, bank=bank: eng.tensor_copy(
                sb["wstage"][:, sl + ct * 128: sl + (ct + 1) * 128], bank[:, 0:128]),
                wait=[(s_pe, s_pe.val)], inc=[(s_ve, _inc(s_ve))])
        p.dma(SYNC, io["y"].ap()[(t * 128 - HALO):(t + 1) * 128 - HALO, :],
              sb["wstage"][:, sl:sl + 1024], d_out, wait=[(s_ve, s_ve.val)])
    for e in ENGINES:
        p.op(e, None, wait=[(d_out, d_out.val)])


def build_kernel_nc():
    return build_program()


def kernel(**inputs):
    sh, per = _prep_inputs(inputs)
    nc = build_program()
    in_maps = []
    for c in range(8):
        m = dict(sh)
        m.update(per[c])
        m["ones_f"] = np.ones((128, 1), np.float32)
        m["onesrow_f"] = np.ones((1, 128), np.float32)
        m["cvals"] = np.broadcast_to(
            np.array([EPS, 0.125, 1.0, 0.0, 1.0 / 256.0, 0, 0, 0], np.float32), (128, 8)).copy()
        in_maps.append(m)
    res = run_bass_kernel_spmd(nc, in_maps, core_ids=list(range(8)))
    out = np.empty((B, T, D), np.float32)
    for c in range(8):
        g, s = c // 4, c % 4
        out[g, s * SPAN:(s + 1) * SPAN] = res.results[c]["y"]
    return out
